# version 01 (final)
# baseline (speedup 1.0000x reference)
import numpy as np
import ml_dtypes

import concourse.bass as bass
import concourse.mybir as mybir
from concourse.bass_utils import run_bass_kernel_spmd

F32 = mybir.dt.float32
BF16 = mybir.dt.bfloat16
AF = mybir.ActivationFunctionType
ALU = mybir.AluOpType
BF = ml_dtypes.bfloat16

D = 2048
NCH = 16
INTER = 8192
NICH = 64
TB = 512
TOWN = 1024
NH = 4
KVR = 512
SCALE = 128.0 ** -0.5
EXP_SHIFT = 6.0
HADS1 = (2.0 ** 8) / (4.0 * 64.0 * float(np.sqrt(128.0)))
HADS2 = (2.0 ** 8) / float(np.sqrt(128.0))

ENGINES = ("tensor", "vector", "scalar", "sync", "gpsimd")


def _had(n):
    h = np.array([[1.0]], dtype=np.float64)
    while h.shape[0] < n:
        h = np.block([[h, h], [h, -h]])
    return h


class Prog:
    def __init__(self):
        self.q = {e: [] for e in ENGINES}
        self.c = {}

    def cnt(self, sem):
        return self.c.get(sem, 0)

    def op(self, eng, meth, *args, waits=(), inc=None, **kwargs):
        if inc is not None:
            amt = inc[1] if inc[1] is not None else 1
            self.c[inc[0]] = self.c.get(inc[0], 0) + amt
        waits = tuple((s, v) for (s, v) in waits if v > 0)

        import traceback
        tag = "".join(traceback.format_stack()[-8:-1])

        def fn(h, sems, meth=meth, args=args, kwargs=kwargs, waits=waits, inc=inc, tag=tag):
            for s, v in waits:
                h.wait_ge(sems[s], v)
            try:
                o = getattr(h, meth)(*args, **kwargs)
            except Exception:
                print("FAILING OP:", meth, "emitted at:\n", tag)
                raise
            if inc is not None:
                if inc[1] is None:
                    o.then_inc(sems[inc[0]])
                else:
                    o.then_inc(sems[inc[0]], inc[1])

        self.q[eng].append(fn)
        return self.c[inc[0]] if inc is not None else None

    def wait(self, eng, sem, val):
        if val <= 0:
            return

        def fn(h, sems, sem=sem, val=val):
            h.wait_ge(sems[sem], val)

        self.q[eng].append(fn)


def build_nc(stage="full"):
    nc = bass.Bass("TRN2", target_bir_lowering=False)

    xin = nc.dram_tensor("xin", [D, TOWN], F32, kind="ExternalInput").ap()
    wupT = nc.dram_tensor("wupT", [D, INTER], F32, kind="ExternalInput").ap()
    wgtT = nc.dram_tensor("wgtT", [D, INTER], F32, kind="ExternalInput").ap()
    wdnT = nc.dram_tensor("wdnT", [INTER, D], F32, kind="ExternalInput").ap()
    wdT = nc.dram_tensor("wdT", [D, KVR], F32, kind="ExternalInput").ap()
    wqTs = nc.dram_tensor("wqTs", [D, NH * 128], F32, kind="ExternalInput").ap()
    wkTs = nc.dram_tensor("wkTs", [KVR, NH * 128], F32, kind="ExternalInput").ap()
    wvTs = nc.dram_tensor("wvTs", [KVR, NH * 128], F32, kind="ExternalInput").ap()
    woTs = nc.dram_tensor("woTs", [NH * 128, D], F32, kind="ExternalInput").ap()
    su_ug = nc.dram_tensor("su_ug", [128, NCH], F32, kind="ExternalInput").ap()
    su_dp = nc.dram_tensor("su_dp", [128, NICH], F32, kind="ExternalInput").ap()
    ws_up = nc.dram_tensor("ws_up", [128, NICH], F32, kind="ExternalInput").ap()
    ws_gt = nc.dram_tensor("ws_gt", [128, NICH], F32, kind="ExternalInput").ap()
    ws_dp = nc.dram_tensor("ws_dp", [128, NCH], F32, kind="ExternalInput").ap()
    h128_in = nc.dram_tensor("h128_in", [128, 128], BF16, kind="ExternalInput").ap()
    hi64_in = nc.dram_tensor("hi64_in", [128, 128], BF16, kind="ExternalInput").ap()
    tri_in = nc.dram_tensor("tri_in", [128, 128], BF16, kind="ExternalInput").ap()

    y = nc.dram_tensor("y", [D, TOWN], F32, kind="ExternalOutput").ap()

    sc_up = nc.dram_tensor("sc_up", [NICH, 128, NCH * 128], BF16)
    sc_gt = nc.dram_tensor("sc_gt", [NICH, 128, NCH * 128], BF16)
    sc_dn = nc.dram_tensor("sc_dn", [INTER, D], BF16)
    ln2o = nc.dram_tensor("ln2o", [D, TOWN], BF16)
    ln2g = nc.dram_tensor("ln2g", [4 * D, TOWN], BF16)
    pao = nc.dram_tensor("pao", [4 * D, TOWN], BF16)
    rso = nc.dram_tensor("rso", [D, TOWN], BF16)

    p = Prog()
    groups = [[0, 1, 2, 3], [4, 5, 6, 7]]

    import contextlib
    es = contextlib.ExitStack()
    with es:
        E = es.enter_context
        xr = E(nc.sbuf_tensor("xr", [128, NCH, TOWN], F32))
        ar1 = E(nc.sbuf_tensor("ar1", [128, 32768], BF16))
        ar2 = E(nc.sbuf_tensor("ar2", [128, 16384], BF16))
        ar3 = E(nc.sbuf_tensor("ar3", [128, 10240], BF16))
        fwt = E(nc.sbuf_tensor("fwt", [128, 4096], BF16))
        fo = E(nc.sbuf_tensor("fo", [128, 2048], BF16))
        pob = E(nc.sbuf_tensor("pob", [128, 2, 512], BF16))
        gsb = E(nc.sbuf_tensor("gsb", [128, 2, 512], BF16))
        xsq = E(nc.sbuf_tensor("xsq", [128, 2, 512], BF16))
        srow = E(nc.sbuf_tensor("srow", [1, 3, 512], F32))
        tsu = E(nc.sbuf_tensor("tsu", [128, NCH], F32))
        tsudp = E(nc.sbuf_tensor("tsudp", [128, NICH], F32))
        tsup = E(nc.sbuf_tensor("tsup", [128, NICH], F32))
        tsg = E(nc.sbuf_tensor("tsg", [128, NICH], F32))
        twsdp = E(nc.sbuf_tensor("twsdp", [128, NCH], F32))
        h128 = E(nc.sbuf_tensor("h128", [128, 128], BF16))
        hi64 = E(nc.sbuf_tensor("hi64", [128, 128], BF16))
        tri = E(nc.sbuf_tensor("tri", [128, 128], BF16))
        onesb = E(nc.sbuf_tensor("onesb", [128, 1], BF16))
        onesf = E(nc.sbuf_tensor("onesf", [128, 1], F32))
        onesrf = E(nc.sbuf_tensor("onesrf", [1, 128], F32))
        negc = E(nc.sbuf_tensor("negc", [128, 1], F32))
        epsr = E(nc.sbuf_tensor("epsr", [1, 1], F32))
        pbs = [E(nc.psum_tensor(f"pb{i}", [128, 512], F32)) for i in range(8)]
        pb0, pb1, pb2, pb3, pb4, pb5, pb6, pb7 = pbs
        DMA_SEMS = ["d_pre", "sl0", "sl1", "sl2", "wt0", "wt1", "wt2", "wd0", "wd1",
                    "fl0", "fl1", "fst", "lno", "gt0", "gt1", "wr0", "wr1", "wat",
                    "po0", "po1", "rt0", "rt1", "out"]
        sems = {}
        for n in DMA_SEMS + ["pe", "dv", "ac", "ag", "rs"]:
            sems[n] = E(nc.semaphore(n))
        block = E(nc.Block())
        pb = [pb0, pb1, pb2, pb3, pb4, pb5, pb6, pb7]
        guard = [("dv", 0)] * 8

        def wait_all_dma(eng):
            for n_ in DMA_SEMS:
                p.wait(eng, n_, p.cnt(n_))

        z_v = ar1.rearrange("p (k t) -> p k t", k=NICH)
        qsb = ar1[:, 0:16384].rearrange("p (h b t) -> p h b t", h=NH, b=8)
        vsb = ar1[:, 16384:32768].rearrange("p (k t) -> p k t", k=32)
        ya = ar2[:, 0:8192].rearrange("p (c t) -> p c t", c=NCH)
        x1 = ar2[:, 8192:16384].rearrange("p (c t) -> p c t", c=NCH)
        ksb = ar2.rearrange("p (h b t) -> p h b t", h=NH, b=8)
        slab = ar3[:, 0:6144].rearrange("p (s k i) -> p s k i", s=3, k=NCH)
        wd_sl = ar3[:, 6144:8192].rearrange("p (s t) -> p s t", s=2)
        wq_r = ar3[:, 0:8192].rearrange("p (c t) -> p c t", c=NCH)
        gt_sl = ar3[:, 8192:9216].rearrange("p (s t) -> p s t", s=2)
        wdr = ar3[:, 0:1024].rearrange("p (s t) -> p s t", s=2)
        wk_r = ar3[:, 1024:3072].rearrange("p (k t) -> p k t", k=4)
        wv_r = ar3[:, 3072:5120].rearrange("p (k t) -> p k t", k=4)
        csb = ar3[:, 5120:7168].rearrange("p (k t) -> p k t", k=4)
        wo_r = ar3[:, 0:8192].rearrange("p (h t) -> p h t", h=NH)
        rt_sl = ar3[:, 0:2048].rearrange("p (s t) -> p s t", s=2)
        pp = fwt.rearrange("p (s t) -> p s t", s=8)
        osb = fo.rearrange("p (h t) -> p h t", h=NH)
        fwt_v = fwt.rearrange("p (s t) -> p s t", s=2)
        fo_j = fo.rearrange("p (j d) -> p j d", j=2)

        xin_r = xin.rearrange("(c p) t -> p c t", p=128)
        p.op("sync", "dma_start", out=xr[:, 0:8, :], in_=xin_r[:, 0:8, :], inc=("d_pre", 16))
        p.op("sync", "dma_start", out=xr[:, 8:16, :], in_=xin_r[:, 8:16, :], inc=("d_pre", 16))
        for dst, src in ((tsu, su_ug), (tsudp, su_dp), (tsup, ws_up), (tsg, ws_gt), (twsdp, ws_dp)):
            p.op("sync", "dma_start", out=dst[:], in_=src, inc=("d_pre", 16))
        for dst, src in ((h128, h128_in), (hi64, hi64_in), (tri, tri_in)):
            p.op("sync", "dma_start", out=dst[:], in_=src, inc=("d_pre", 16))
        d_pre = p.cnt("d_pre")

        p.op("vector", "memset", onesb[:], 1.0)
        p.op("vector", "memset", onesf[:], 1.0)
        p.op("vector", "memset", onesrf[:], 1.0)
        p.op("vector", "memset", negc[:], -EXP_SHIFT)
        p.op("vector", "memset", epsr[:], 1e-5)
        p.op("vector", "tensor_scalar", tsup[:], tsup[:], 64.0, None, ALU.mult,
             waits=(("d_pre", d_pre),))
        p.op("vector", "tensor_scalar", tsg[:], tsg[:], 64.0, None, ALU.mult)
        p.op("vector", "tensor_scalar", twsdp[:], twsdp[:], 32.0, None, ALU.mult)
        dv_tabs = p.op("vector", "tensor_scalar", tsudp[:], tsudp[:], 1.0 / 64.0, None,
                       ALU.mult, inc=("dv", 1))

        wdnT_v = wdnT.rearrange("(a b) d -> a b d", b=128)
        sc_dn_v = sc_dn.ap().rearrange("(a b) d -> a b d", b=128)
        fold = {"load_done": [0, 0], "mm_done": [0, 0], "evac_done": 0, "store_done": 0}

        def fold_step(bp):
            slot = bp % 2
            fsem = "fl%d" % slot
            p.wait("gpsimd", "pe", fold["mm_done"][slot])
            fold["load_done"][slot] = p.op(
                "gpsimd", "dma_start", out=fwt_v[:, slot, :],
                in_=wdnT_v[:, 2 * bp:2 * bp + 2, :], inc=(fsem, 16))
            for db in range(4):
                bk = 6 + (db % 2)
                p.wait("tensor", fsem, fold["load_done"][slot])
                p.wait("tensor", guard[bk][0], guard[bk][1])
                mm = p.op("tensor", "matmul", pb[bk][:, :], hi64[:],
                          fwt_v[:, slot, db * 512:(db + 1) * 512],
                          start=True, stop=True, inc=("pe", 1))
                if db == 3:
                    fold["mm_done"][slot] = mm
                p.wait("vector", "fst", fold["store_done"])
                ev = p.op("vector", "tensor_copy", fo[:, db * 512:(db + 1) * 512], pb[bk][:, :],
                          waits=(("pe", mm),), inc=("dv", 1))
                guard[bk] = ("dv", ev)
                fold["evac_done"] = ev
            p.wait("sync", "dv", fold["evac_done"])
            fold["store_done"] = p.op(
                "sync", "dma_start", out=sc_dn_v[:, 2 * bp:2 * bp + 2, :], in_=fo[:, :],
                inc=("fst", 16))

        state = {"xsq_dv": 0, "xsq_pe": 0}

        def emit_ln(tbl, with_su, xwait_dv=0, ya_free=None):
            xs = lambda c: xr[:, c, tbl * TB:(tbl + 1) * TB]
            p.wait("tensor", guard[0][0], guard[0][1])
            p.wait("tensor", guard[1][0], guard[1][1])
            if xwait_dv:
                p.wait("tensor", "dv", xwait_dv)
                p.wait("scalar", "dv", xwait_dv)
            p.wait("tensor", "dv", dv_tabs)
            p.wait("tensor", "d_pre", d_pre)
            p.wait("scalar", "d_pre", d_pre)
            p.wait("scalar", "dv", state["xsq_dv"])
            p.wait("scalar", "pe", state["xsq_pe"])
            mm2_done = [0] * 16
            for c in range(16):
                w = (("pe", mm2_done[c - 2]),) if c >= 2 else ()
                sq = p.op("scalar", "activation", xsq[:, c % 2, :], xs(c), AF.Square,
                          waits=w, inc=("ac", 1))
                p.op("tensor", "matmul", pb[0][0:1, :], onesf[:], xs(c),
                     start=(c == 0), stop=(c == 15), inc=("pe", 1))
                mm2_done[c] = p.op("tensor", "matmul", pb[1][0:1, :], onesb[:], xsq[:, c % 2, :],
                                   start=(c == 0), stop=(c == 15),
                                   waits=(("ac", sq),), inc=("pe", 1))
            pe_sums = p.cnt("pe")
            state["xsq_pe"] = mm2_done[15]
            ac_m = p.op("scalar", "mul", srow[0:1, 0, :], pb[0][0:1, :], 1.0 / D,
                        waits=(("pe", pe_sums),), inc=("ac", 1))
            dv_msq = p.op("vector", "tensor_tensor", srow[0:1, 1, :], srow[0:1, 0, :],
                          srow[0:1, 0, :], ALU.mult, waits=(("ac", ac_m),), inc=("dv", 1))
            dv_var = p.op("vector", "scalar_tensor_tensor", srow[0:1, 1, :], pb[1][0:1, :],
                          1.0 / D, srow[0:1, 1, :], ALU.mult, ALU.subtract, inc=("dv", 1))
            guard[0] = ("ac", ac_m)
            guard[1] = ("dv", dv_var)
            ac_sd = p.op("scalar", "activation", srow[0:1, 2, :], srow[0:1, 1, :], AF.Sqrt,
                         bias=epsr[:], waits=(("dv", dv_var),), inc=("ac", 1))
            dv_rstd = p.op("vector", "reciprocal", srow[0:1, 1, :], srow[0:1, 2, :],
                           waits=(("ac", ac_sd),), inc=("dv", 1))
            p.wait("tensor", guard[2][0], guard[2][1])
            p.wait("tensor", guard[3][0], guard[3][1])
            p.op("tensor", "matmul", pb[2][:, :], onesrf[:], srow[0:1, 0, :],
                 start=True, stop=True, waits=(("dv", dv_msq),), inc=("pe", 1))
            pe_bc = p.op("tensor", "matmul", pb[3][:, :], onesrf[:], srow[0:1, 1, :],
                         start=True, stop=True, waits=(("dv", dv_rstd),), inc=("pe", 1))
            last = 0
            for c in range(16):
                w = (("pe", pe_bc),) if c == 0 else ()
                if c == 0 and ya_free is not None:
                    w = w + (ya_free,)
                p.op("vector", "tensor_tensor", ya[:, c, :], xs(c), pb[2][:, :], ALU.subtract,
                     waits=w, inc=("dv", 1))
                if with_su:
                    last = p.op("vector", "scalar_tensor_tensor", ya[:, c, :], ya[:, c, :],
                                tsu[:, c:c + 1], pb[3][:, :], ALU.mult, ALU.mult, inc=("dv", 1))
                else:
                    last = p.op("vector", "tensor_tensor", ya[:, c, :], ya[:, c, :],
                                pb[3][:, :], ALU.mult, inc=("dv", 1))
            guard[2] = ("dv", last)
            guard[3] = ("dv", last)
            return last

        def emit_butterfly():
            last = 0
            for s in range(4):
                src = ya if s % 2 == 0 else x1
                dst = x1 if s % 2 == 0 else ya
                g = 16 // (2 ** (s + 1))
                sv = src.rearrange("p (g two inner) t -> p g two (inner t)", g=g, two=2)
                dvv = dst.rearrange("p (g two inner) t -> p g two (inner t)", g=g, two=2)
                p.op("vector", "tensor_tensor", dvv[:, :, 0, :], sv[:, :, 0, :], sv[:, :, 1, :],
                     ALU.add, inc=("dv", 1))
                last = p.op("vector", "tensor_tensor", dvv[:, :, 1, :], sv[:, :, 0, :],
                            sv[:, :, 1, :], ALU.subtract, inc=("dv", 1))
            return last

        def emit_h128(bf_done):
            lastdv = 0
            for c in range(16):
                bk = 4 + (c % 2)
                p.wait("tensor", guard[bk][0], guard[bk][1])
                w = (("dv", bf_done),) if c == 0 else ()
                mm = p.op("tensor", "matmul", pb[bk][:, :], h128[:], ya[:, c, :],
                          start=True, stop=True, waits=w, inc=("pe", 1))
                lastdv = p.op("vector", "tensor_scalar", x1[:, c, :], pb[bk][:, :], HADS1,
                              None, ALU.mult, waits=(("pe", mm),), inc=("dv", 1))
                guard[bk] = ("dv", lastdv)
            return lastdv

        wup_v = wupT.rearrange("(k p) i -> p k i", p=128)
        wgt_v = wgtT.rearrange("(k p) i -> p k i", p=128)
        scu_v = sc_up.ap()
        scg_v = sc_gt.ap()
        scd_v = sc_dn.ap().rearrange("(k p) d -> p k d", p=128)

        def emit_ffn_tb(call, tbl, xwait_dv=0, region_pe=0, region_dv=0):
            region_pe = max(region_pe, p.cnt("pe"))
            ln_done = emit_ln(tbl, with_su=True, xwait_dv=xwait_dv)
            bf_done = emit_butterfly()
            x1_done = emit_h128(bf_done)
            raw = (call == 1 and tbl == 0)
            sl_load = [0, 0, 0]
            sl_sem = ["sl0", "sl1", "sl2"]
            sl_mm = [0, 0, 0]
            sl_store = [0, 0, 0]
            first_scratch = [not raw]
            gs_guard = [0, 0]
            pob_pe = [0, 0]
            z_last = 0

            def load_slab(slot, which, i):
                cols = slice(i * 128, (i + 1) * 128)
                ssem = "sl%d" % slot
                wsem = "wt%d" % slot
                if raw:
                    src = (wup_v if which == 0 else wgt_v)[:, :, cols]
                    p.wait("gpsimd", "pe", max(sl_mm[slot], region_pe))
                    p.wait("gpsimd", wsem, sl_store[slot])
                    if region_dv:
                        p.wait("gpsimd", "dv", region_dv)
                    n1 = p.op("gpsimd", "dma_start", out=slab[:, slot, 0:8, :],
                              in_=src[:, 0:8, :], inc=(ssem, 16))
                    n2 = p.op("gpsimd", "dma_start", out=slab[:, slot, 8:16, :],
                              in_=src[:, 8:16, :], inc=(ssem, 16))
                    sl_sem[slot] = ssem
                    sl_load[slot] = n2
                    dst = (scu_v if which == 0 else scg_v)[i]
                    p.wait("sync", ssem, n2)
                    sl_store[slot] = p.op(
                        "sync", "dma_start", out=dst,
                        in_=slab[:, slot, :, :], inc=(wsem, 16))
                else:
                    src = (scu_v if which == 0 else scg_v)[i]
                    p.wait("sync", "pe", max(sl_mm[slot], region_pe))
                    if region_dv:
                        p.wait("sync", "dv", region_dv)
                    if first_scratch[0]:
                        for w_ in ("wt0", "wt1", "wt2"):
                            p.wait("sync", w_, p.cnt(w_))
                        first_scratch[0] = False
                    n = p.op("sync", "dma_start", out=slab[:, slot, :, :], in_=src,
                             inc=(ssem, 16))
                    sl_sem[slot] = ssem
                    sl_load[slot] = n

            load_slab(0, 0, 0)
            load_slab(1, 1, 0)
            for i in range(NICH):
                s_up = (2 * i) % 3
                s_gt = (2 * i + 1) % 3
                s_nxt = (2 * i + 2) % 3
                bk_u = i % 2
                bk_g = 2 + i % 2
                bk_z = 4 + i % 2
                slot = i % 2
                p.wait("tensor", guard[bk_u][0], guard[bk_u][1])
                p.wait("tensor", sl_sem[s_up], sl_load[s_up])
                if i == 0:
                    p.wait("tensor", "dv", x1_done)
                mm_u = 0
                for k in range(16):
                    mm_u = p.op("tensor", "matmul", pb[bk_u][:, :], slab[:, s_up, k, :],
                                x1[:, k, :], start=(k == 0), stop=(k == 15), inc=("pe", 1))
                sl_mm[s_up] = mm_u
                if i + 1 < NICH:
                    load_slab(s_nxt, 0, i + 1)
                p.wait("tensor", guard[bk_g][0], guard[bk_g][1])
                p.wait("tensor", sl_sem[s_gt], sl_load[s_gt])
                mm_g = 0
                for k in range(16):
                    mm_g = p.op("tensor", "matmul", pb[bk_g][:, :], slab[:, s_gt, k, :],
                                x1[:, k, :], start=(k == 0), stop=(k == 15), inc=("pe", 1))
                sl_mm[s_gt] = mm_g
                if i + 1 < NICH:
                    load_slab(s_up, 1, i + 1)
                dv_us = p.op("vector", "tensor_scalar", xsq[:, slot, :], pb[bk_u][:, :],
                             tsup[:, i:i + 1], None, ALU.mult,
                             waits=(("pe", mm_u),), inc=("dv", 1))
                guard[bk_u] = ("dv", dv_us)
                ac_gs = p.op("scalar", "activation", gsb[:, slot, :], pb[bk_g][:, :], AF.Silu,
                             scale=tsg[:, i:i + 1],
                             waits=(("pe", mm_g), ("dv", gs_guard[slot])), inc=("ac", 1))
                guard[bk_g] = ("ac", ac_gs)
                dv_hh = p.op("vector", "scalar_tensor_tensor", pob[:, slot, :], xsq[:, slot, :],
                             tsudp[:, i:i + 1], gsb[:, slot, :], ALU.mult, ALU.mult,
                             waits=(("ac", ac_gs), ("pe", pob_pe[slot])), inc=("dv", 1))
                gs_guard[slot] = dv_hh
                state["xsq_dv"] = dv_hh
                p.wait("tensor", guard[bk_z][0], guard[bk_z][1])
                mm_z = p.op("tensor", "matmul", pb[bk_z][:, :], h128[:], pob[:, slot, :],
                            start=True, stop=True, waits=(("dv", dv_hh),), inc=("pe", 1))
                pob_pe[slot] = mm_z
                z_last = p.op("vector", "tensor_scalar", z_v[:, i, :], pb[bk_z][:, :], HADS2,
                              None, ALU.mult, waits=(("pe", mm_z),), inc=("dv", 1))
                guard[bk_z] = ("dv", z_last)
                if call == 1 and tbl == 0:
                    fold_step(i)

            wd_mm = [0, 0]
            resid_last = 0
            if call == 1 and tbl == 0:
                p.wait("sync", "fst", fold["store_done"])
            for grp in range(2):
                cols = slice(grp * 1024, (grp + 1) * 1024)
                ld = [0, 0]
                for k in range(NICH):
                    slot = k % 2
                    wsem = "wd%d" % slot
                    p.wait("sync", "pe", wd_mm[slot])
                    ld[slot] = p.op("sync", "dma_start", out=wd_sl[:, slot, :],
                                    in_=scd_v[:, k, cols], inc=(wsem, 16))
                    if k == 0:
                        for m in range(8):
                            p.wait("tensor", guard[m][0], guard[m][1])
                        p.wait("tensor", "dv", z_last)
                    p.wait("tensor", wsem, ld[slot])
                    mm = 0
                    for m in range(8):
                        mm = p.op("tensor", "matmul", pb[m][:, :],
                                  wd_sl[:, slot, m * 128:(m + 1) * 128], z_v[:, k, :],
                                  start=(k == 0), stop=(k == NICH - 1), inc=("pe", 1))
                    wd_mm[slot] = mm
                pe_dn = p.cnt("pe")
                for m in range(8):
                    dch = grp * 8 + m
                    resid_last = p.op(
                        "vector", "scalar_tensor_tensor", xr[:, dch, tbl * TB:(tbl + 1) * TB],
                        pb[m][:, :], twsdp[:, dch:dch + 1],
                        xr[:, dch, tbl * TB:(tbl + 1) * TB], ALU.mult, ALU.add,
                        waits=(("pe", pe_dn),), inc=("dv", 1))
                    guard[m] = ("dv", resid_last)
            return resid_last

        def emit_attn(resid0, resid1, stop_after_sweeps=False):
            ln2o_v = ln2o.ap().rearrange("(c p) t -> p c t", p=128)
            st_ln = 0
            r_by_tbl = (resid0, resid1)
            for tbl in range(2):
                last = emit_ln(tbl, with_su=False, xwait_dv=r_by_tbl[tbl],
                               ya_free=(("lno", st_ln) if tbl else None))
                p.wait("sync", "dv", last)
                st_ln = p.op("sync", "dma_start",
                             out=ln2o_v[:, :, tbl * TB:(tbl + 1) * TB], in_=ya[:, :, :],
                             inc=("lno", 16))
            p.wait("gpsimd", "lno", st_ln)
            p.op("gpsimd", "collective_compute", "AllGather", ALU.bypass,
                 replica_groups=groups, ins=[ln2o.ap()], outs=[ln2g.ap()],
                 inc=("ag", None))
            ag_done = 1

            pe_snap = p.cnt("pe")
            dv_snap = p.cnt("dv")
            p.wait("gpsimd", "pe", pe_snap)
            p.wait("gpsimd", "dv", dv_snap)
            wq_ld = p.op("gpsimd", "dma_start", out=wq_r[:, :, :],
                         in_=wqTs.rearrange("(c p) n -> p c n", p=128), inc=("wat", 16))
            gt_pe = [0, 0]
            gt_ld = [0, 0]
            q_dv_last = 0
            p.wait("sync", "ag", ag_done)
            for tb in range(8):
                for c in range(4):
                    p.wait("tensor", guard[c][0], guard[c][1])
                for dch in range(16):
                    slot = dch % 2
                    gsem = "gt%d" % slot
                    p.wait("sync", "pe", gt_pe[slot])
                    row0 = (tb // 2) * D + dch * 128
                    col0 = (tb % 2) * TB
                    gt_ld[slot] = p.op("sync", "dma_start", out=gt_sl[:, slot, :],
                                       in_=ln2g.ap()[row0:row0 + 128, col0:col0 + TB],
                                       inc=(gsem, 16))
                    w = ((gsem, gt_ld[slot]),)
                    if dch == 0:
                        w = w + (("wat", wq_ld),)
                    mmq = 0
                    for h in range(NH):
                        mmq = p.op("tensor", "matmul", pb[h][:, :],
                                   wq_r[:, dch, h * 128:(h + 1) * 128], gt_sl[:, slot, :],
                                   start=(dch == 0), stop=(dch == 15),
                                   waits=w if h == 0 else (), inc=("pe", 1))
                    gt_pe[slot] = mmq
                for h in range(NH):
                    q_dv_last = p.op("vector", "tensor_copy", qsb[:, h, tb, :], pb[h][:, :],
                                     waits=(("pe", gt_pe[0 if (15 % 2 == 0) else 1]),) if h == 0 else (),
                                     inc=("dv", 1))
                    guard[h] = ("dv", q_dv_last)

            pe_snap2 = p.cnt("pe")
            p.wait("gpsimd", "pe", pe_snap2)
            wk_ld = p.op("gpsimd", "dma_start", out=wk_r[:, :, :],
                         in_=wkTs.rearrange("(c p) n -> p c n", p=128), inc=("wat", 16))
            wv_ld = p.op("gpsimd", "dma_start", out=wv_r[:, :, :],
                         in_=wvTs.rearrange("(c p) n -> p c n", p=128), inc=("wat", 16))
            wdT_v = wdT.rearrange("(c p) n -> p c n", p=128)
            wdr_ld = [0, 0]
            wdr_pe = [0, 0]
            csb_pe = 0
            kv_dv_last = 0
            for tb in range(8):
                for c in range(4):
                    p.wait("tensor", guard[c][0], guard[c][1])
                for dch in range(16):
                    slot = dch % 2
                    gsem = "gt%d" % slot
                    rsem = "wr%d" % slot
                    p.wait("sync", "pe", gt_pe[slot])
                    row0 = (tb // 2) * D + dch * 128
                    col0 = (tb % 2) * TB
                    gt_ld[slot] = p.op("sync", "dma_start", out=gt_sl[:, slot, :],
                                       in_=ln2g.ap()[row0:row0 + 128, col0:col0 + TB],
                                       inc=(gsem, 16))
                    p.wait("gpsimd", "pe", wdr_pe[slot])
                    wdr_ld[slot] = p.op("gpsimd", "dma_start", out=wdr[:, slot, :],
                                        in_=wdT_v[:, dch, :], inc=(rsem, 16))
                    w = ((gsem, gt_ld[slot]), (rsem, wdr_ld[slot]))
                    mmc = 0
                    for kv in range(4):
                        mmc = p.op("tensor", "matmul", pb[kv][:, :],
                                   wdr[:, slot, kv * 128:(kv + 1) * 128], gt_sl[:, slot, :],
                                   start=(dch == 0), stop=(dch == 15),
                                   waits=w if kv == 0 else (), inc=("pe", 1))
                    gt_pe[slot] = mmc
                    wdr_pe[slot] = mmc
                pe_c = p.cnt("pe")
                c_dv = 0
                for kv in range(4):
                    c_dv = p.op("vector", "tensor_copy", csb[:, kv, :], pb[kv][:, :],
                                waits=(("pe", max(pe_c, csb_pe)),) if kv == 0 else (),
                                inc=("dv", 1))
                    guard[kv] = ("dv", c_dv)
                for h in range(NH):
                    bkk = 4 + (h % 2)
                    p.wait("tensor", guard[bkk][0], guard[bkk][1])
                    w = (("dv", c_dv), ("wat", wv_ld))
                    mmk = 0
                    for kv in range(4):
                        mmk = p.op("tensor", "matmul", pb[bkk][:, :],
                                   wk_r[:, kv, h * 128:(h + 1) * 128], csb[:, kv, :],
                                   start=(kv == 0), stop=(kv == 3),
                                   waits=w if kv == 0 else (), inc=("pe", 1))
                    kv_dv_last = p.op("vector", "tensor_copy", ksb[:, h, tb, :], pb[bkk][:, :],
                                      waits=(("pe", mmk),), inc=("dv", 1))
                    guard[bkk] = ("dv", kv_dv_last)
                for tkc in range(4):
                    bkv = 6 + (tkc % 2)
                    p.wait("tensor", guard[bkv][0], guard[bkv][1])
                    mmv = 0
                    for kv in range(4):
                        mmv = p.op("tensor", "matmul", pb[bkv][:, :],
                                   csb[:, kv, tkc * 128:(tkc + 1) * 128], wv_r[:, kv, :],
                                   start=(kv == 0), stop=(kv == 3), inc=("pe", 1))
                    kv_dv_last = p.op("vector", "tensor_copy", vsb[:, tb * 4 + tkc, :],
                                      pb[bkv][:, :], waits=(("pe", mmv),), inc=("dv", 1))
                    guard[bkv] = ("dv", kv_dv_last)
                csb_pe = p.cnt("pe")

            if stop_after_sweeps:
                return 0
            pe_snap3 = p.cnt("pe")
            dv_snap3 = p.cnt("dv")
            p.wait("gpsimd", "pe", pe_snap3)
            wo_ld = p.op("gpsimd", "dma_start", out=wo_r[:, :, :],
                         in_=woTs.rearrange("(h p) d -> p h d", p=128), inc=("wat", 16))
            ppz = p.op("vector", "memset", fwt[:, :], 0.0, waits=(("pe", pe_snap3),),
                       inc=("dv", 1))
            po_st = [0, 0]
            osb_pe = 0
            first_flash = True
            for qb in range(8):
                osb_dv = [0, 0, 0, 0]
                for h in range(NH):
                    bkO = 4 if h % 2 == 0 else 6
                    bkR = 5 if h % 2 == 0 else 7
                    p.wait("tensor", guard[bkO][0], guard[bkO][1])
                    p.wait("tensor", guard[bkR][0], guard[bkR][1])
                    first = True
                    pv_mm = rs_mm = 0
                    for j in range(qb + 1):
                        diag = (j == qb)
                        for c4 in range(4):
                            cs = slice(c4 * 128, 512) if diag else slice(0, 512)
                            bkS = c4 % 2
                            p.wait("tensor", guard[bkS][0], guard[bkS][1])
                            w = ()
                            if first_flash:
                                w = (("dv", max(dv_snap3, ppz)),)
                                first_flash = False
                            mm_s = p.op("tensor", "matmul", pb[bkS][:, cs],
                                        ksb[:, h, j, c4 * 128:(c4 + 1) * 128],
                                        qsb[:, h, qb, cs], start=True, stop=True,
                                        waits=w, inc=("pe", 1))
                            slot = (4 + c4) if diag else c4
                            ew = (("pe", mm_s),)
                            if first:
                                ew = ew + (("dv", ppz),)
                            ac_e = p.op("scalar", "activation", pp[:, slot, cs],
                                        pb[bkS][:, cs], AF.Exp, bias=negc[:], scale=SCALE,
                                        waits=ew, inc=("ac", 1))
                            guard[bkS] = ("ac", ac_e)
                            pvw = (("ac", ac_e),)
                            if diag:
                                dv_m = p.op("vector", "tensor_tensor",
                                            pp[:, slot, c4 * 128:(c4 + 1) * 128],
                                            pp[:, slot, c4 * 128:(c4 + 1) * 128], tri[:],
                                            ALU.mult, waits=(("ac", ac_e),), inc=("dv", 1))
                                pvw = (("dv", dv_m),)
                            last_mm = diag and (c4 == 3)
                            pv_mm = p.op("tensor", "matmul", pb[bkO][:, :],
                                         vsb[:, j * 4 + c4, h * 128:(h + 1) * 128],
                                         pp[:, slot, :], start=first, stop=last_mm,
                                         waits=pvw, inc=("pe", 1))
                            rs_mm = p.op("tensor", "matmul", pb[bkR][0:1, :], onesb[:],
                                         pp[:, slot, :], start=first, stop=last_mm,
                                         inc=("pe", 1))
                            first = False
                    dv_rec = p.op("vector", "reciprocal", srow[0:1, 2, :], pb[bkR][0:1, :],
                                  waits=(("pe", rs_mm),), inc=("dv", 1))
                    guard[bkR] = ("dv", dv_rec)
                    p.wait("tensor", guard[2][0], guard[2][1])
                    mm_bc = p.op("tensor", "matmul", pb[2][:, :], onesrf[:], srow[0:1, 2, :],
                                 start=True, stop=True, waits=(("dv", dv_rec),), inc=("pe", 1))
                    ac_o = p.op("scalar", "copy", xsq[:, h % 2, :], pb[bkO][:, :],
                                waits=(("pe", pv_mm), ("dv", osb_dv[h - 2] if h > 1 else 0)),
                                inc=("ac", 1))
                    guard[bkO] = ("ac", ac_o)
                    dv_o = p.op("vector", "tensor_tensor", osb[:, h, :], xsq[:, h % 2, :],
                                pb[2][:, :], ALU.mult,
                                waits=(("ac", ac_o), ("pe", max(mm_bc, osb_pe))),
                                inc=("dv", 1))
                    guard[2] = ("dv", dv_o)
                    osb_dv[h] = dv_o
                wo_mm_last = 0
                for dch in range(16):
                    bkW = dch % 2
                    p.wait("tensor", guard[bkW][0], guard[bkW][1])
                    mm = 0
                    for h in range(NH):
                        w = (("dv", osb_dv[h]),)
                        if dch == 0 and h == 0:
                            w = w + (("wat", wo_ld),)
                        mm = p.op("tensor", "matmul", pb[bkW][:, :],
                                  wo_r[:, h, dch * 128:(dch + 1) * 128], osb[:, h, :],
                                  start=(h == 0), stop=(h == 3), waits=w, inc=("pe", 1))
                    psem = "po%d" % (dch % 2)
                    dv_po = p.op("vector", "tensor_copy", pob[:, dch % 2, :], pb[bkW][:, :],
                                 waits=(("pe", mm), (psem, po_st[dch % 2])), inc=("dv", 1))
                    guard[bkW] = ("dv", dv_po)
                    p.wait("sync", "dv", dv_po)
                    r0 = (qb // 2) * D + dch * 128
                    c0 = (qb % 2) * TB
                    po_st[dch % 2] = p.op("sync", "dma_start",
                                          out=pao.ap()[r0:r0 + 128, c0:c0 + TB],
                                          in_=pob[:, dch % 2, :], inc=(psem, 16))
                    wo_mm_last = mm
                osb_pe = wo_mm_last

            p.wait("gpsimd", "po0", po_st[0])
            p.wait("gpsimd", "po1", po_st[1])
            p.op("gpsimd", "collective_compute", "ReduceScatter", ALU.add,
                 replica_groups=groups, ins=[pao.ap()], outs=[rso.ap()], inc=("rs", None))
            rt_ld = [0, 0]
            rt_dv = [0, 0]
            resid = 0
            p.wait("sync", "rs", 1)
            p.wait("sync", "pe", p.cnt("pe"))
            for dch in range(16):
                slot = dch % 2
                tsem = "rt%d" % slot
                p.wait("sync", "dv", rt_dv[slot])
                rt_ld[slot] = p.op("sync", "dma_start", out=rt_sl[:, slot, :],
                                   in_=rso.ap()[dch * 128:(dch + 1) * 128, :],
                                   inc=(tsem, 16))
                resid = p.op("vector", "tensor_tensor", xr[:, dch, :], xr[:, dch, :],
                             rt_sl[:, slot, :], ALU.add,
                             waits=((tsem, rt_ld[slot]),), inc=("dv", 1))
                rt_dv[slot] = resid
            return resid

        if stage.startswith("pao") or stage in ("qkv", "ln2o", "ln2g"):
            r1 = emit_ffn_tb(1, 0)
            r1b = emit_ffn_tb(1, 1)
            if stage in ("ln2o", "ln2g"):
                emit_attn(r1, r1b, stop_after_sweeps=True)
                p.wait("gpsimd", "dv", p.cnt("dv"))
                p.wait("gpsimd", "pe", p.cnt("pe"))
                wait_all_dma("gpsimd")
                src_ap = ln2o.ap() if stage == "ln2o" else ln2g.ap()[0:D, :]
                p.op("gpsimd", "dma_start", out=y, in_=src_ap, inc=("out", 16))
            elif stage == "qkv":
                emit_attn(r1, r1b, stop_after_sweeps=True)
                p.wait("gpsimd", "dv", p.cnt("dv"))
                p.wait("gpsimd", "pe", p.cnt("pe"))
                yv = y.rearrange("(c p) t -> p c t", p=128)
                p.op("gpsimd", "dma_start", out=yv[:, 0:4, :],
                     in_=qsb[:, 0, :, :], inc=("out", 16))
                p.op("gpsimd", "dma_start", out=yv[:, 4:8, :],
                     in_=ksb[:, 0, :, :], inc=("out", 16))
                p.op("gpsimd", "dma_start", out=yv[:, 8:16, :],
                     in_=vsb[:, 0:16, :], inc=("out", 16))
            else:
                qq = int(stage[3])
                emit_attn(r1, r1b)
                wait_all_dma("gpsimd")
                p.op("gpsimd", "dma_start",
                     out=y, in_=pao.ap()[qq * D:(qq + 1) * D, :], inc=("out", 16))
            p.wait("gpsimd", "out", p.cnt("out"))
            p.wait("sync", "out", p.cnt("out"))

            @block.tensor
            def _(t):
                for fn in p.q["tensor"]:
                    fn(t, sems)

            @block.vector
            def _(v):
                for fn in p.q["vector"]:
                    fn(v, sems)

            @block.scalar
            def _(s):
                for fn in p.q["scalar"]:
                    fn(s, sems)

            @block.sync
            def _(s):
                for fn in p.q["sync"]:
                    fn(s, sems)

            @block.gpsimd
            def _(g):
                for fn in p.q["gpsimd"]:
                    fn(g, sems)

            return nc

        if stage in ("ln", "x1", "z", "f10", "scdn"):
            if stage == "scdn":
                emit_ffn_tb(1, 0)
                wait_all_dma("gpsimd")
                p.wait("gpsimd", "pe", p.cnt("pe"))
                p.op("gpsimd", "dma_start", out=y, in_=sc_dn.ap()[0:D, 0:TOWN],
                     inc=("out", 16))
                p.wait("gpsimd", "out", p.cnt("out"))
                p.wait("sync", "out", p.cnt("out"))
            elif stage == "f10":
                emit_ffn_tb(1, 0)
                p.wait("sync", "dv", p.cnt("dv"))
                p.op("sync", "dma_start",
                     out=y.rearrange("(c p) t -> p c t", p=128)[:, 0:8, :],
                     in_=xr[:, 0:8, :], inc=("out", 16))
                p.op("sync", "dma_start",
                     out=y.rearrange("(c p) t -> p c t", p=128)[:, 8:16, :],
                     in_=xr[:, 8:16, :], inc=("out", 16))
                p.wait("sync", "out", p.cnt("out"))
            elif stage == "z":
                emit_ffn_tb(1, 0)
                p.wait("gpsimd", "dv", p.cnt("dv"))
                p.wait("gpsimd", "pe", p.cnt("pe"))
                p.op("gpsimd", "dma_start",
                     out=y.rearrange("(c p) t -> p c t", p=128)[:, :, 0:TB],
                     in_=z_v[:, 0:16, :], inc=("out", 16))
            else:
                for tbl in range(2):
                    last = emit_ln(tbl, with_su=True,
                                   ya_free=(("out", p.cnt("out")) if tbl else None))
                    if stage == "x1":
                        bf = emit_butterfly()
                        last = emit_h128(bf)
                    src_v = ya if stage == "ln" else x1
                    p.wait("gpsimd", "dv", last)
                    p.op("gpsimd", "dma_start",
                         out=y.rearrange("(c p) t -> p c t", p=128)[:, :, tbl * TB:(tbl + 1) * TB],
                         in_=src_v[:, :, :], inc=("out", 16))
            p.wait("gpsimd", "out", p.cnt("out"))
            p.wait("sync", "out", p.cnt("out"))
            wait_all_dma("sync")

            @block.tensor
            def _(t):
                for fn in p.q["tensor"]:
                    fn(t, sems)

            @block.vector
            def _(v):
                for fn in p.q["vector"]:
                    fn(v, sems)

            @block.scalar
            def _(s):
                for fn in p.q["scalar"]:
                    fn(s, sems)

            @block.sync
            def _(s):
                for fn in p.q["sync"]:
                    fn(s, sems)

            @block.gpsimd
            def _(g):
                for fn in p.q["gpsimd"]:
                    fn(g, sems)

            return nc
        r1 = emit_ffn_tb(1, 0)
        r1b = emit_ffn_tb(1, 1)
        if stage in ("attn", "full"):
            ra = emit_attn(r1, r1b)
            if stage == "full":
                pe_f2 = p.cnt("pe")
                dv_f2 = ra
                r2 = emit_ffn_tb(2, 0, xwait_dv=ra, region_pe=pe_f2, region_dv=dv_f2)
                r2b = emit_ffn_tb(2, 1, xwait_dv=ra)
                out_wait = r2b
            else:
                out_wait = ra
        else:
            out_wait = r1b

        y_v = y.rearrange("(c p) t -> p c t", p=128)
        p.wait("sync", "dv", out_wait)
        p.op("sync", "dma_start", out=y_v[:, 0:8, :], in_=xr[:, 0:8, :], inc=("out", 16))
        p.op("sync", "dma_start", out=y_v[:, 8:16, :], in_=xr[:, 8:16, :], inc=("out", 16))
        p.wait("sync", "out", p.cnt("out"))

        @block.tensor
        def _(t):
            for fn in p.q["tensor"]:
                fn(t, sems)

        @block.vector
        def _(v):
            for fn in p.q["vector"]:
                fn(v, sems)

        @block.scalar
        def _(s):
            for fn in p.q["scalar"]:
                fn(s, sems)

        @block.sync
        def _(s):
            for fn in p.q["sync"]:
                fn(s, sems)

        @block.gpsimd
        def _(g):
            for fn in p.q["gpsimd"]:
                fn(g, sems)

    return nc


def _host_inputs(inp):
    T = lambda a: np.ascontiguousarray(np.asarray(a, np.float32).T)
    vec = lambda a, n: np.ascontiguousarray(
        np.asarray(a, np.float32).reshape(n, 128).T)
    x = np.asarray(inp["L0_attn_x"], np.float32)
    wupT = T(inp["L0_ffn_W_up"])
    wgtT = T(inp["L0_ffn_W_gate"])
    wdnT = T(inp["L0_ffn_W_down"])
    wdT = T(inp["L0_attn_Wd"])
    wqT = T(inp["L0_attn_Wq"])
    wkT = T(inp["L0_attn_Wk"])
    wvT = T(inp["L0_attn_Wv"])
    woT = T(inp["L0_attn_Wo"])
    wsug = np.asarray(inp["L0_ffn_Wscale_ug"], np.float32)
    su_ug = vec(inp["L0_ffn_SU_ug"], NCH)
    su_dp = vec(inp["L0_ffn_SU_dp"], NICH)
    ws_up = vec(wsug[:INTER], NICH)
    ws_gt = vec(wsug[INTER:], NICH)
    ws_dp = vec(inp["L0_ffn_Wscale_dp"], NCH)

    h128c = np.ascontiguousarray((_had(128) * (2.0 ** -8)).astype(BF))
    h64 = _had(64) / 8.0
    hi64 = np.zeros((128, 128), dtype=np.float64)
    for ap_ in range(64):
        for a in range(64):
            hi64[2 * ap_, 2 * a] = h64[a, ap_]
            hi64[2 * ap_ + 1, 2 * a + 1] = h64[a, ap_]
    hi64 = np.ascontiguousarray(hi64.astype(BF))
    tri = np.ascontiguousarray(
        (np.arange(128)[:, None] <= np.arange(128)[None, :]).astype(BF))

    in_maps = []
    for c in range(8):
        b, r = c // 4, c % 4
        in_maps.append(dict(
            xin=np.ascontiguousarray(x[b, r * 1024:(r + 1) * 1024, :].T),
            wupT=wupT, wgtT=wgtT, wdnT=wdnT, wdT=wdT,
            wqTs=np.ascontiguousarray(wqT[:, r * 512:(r + 1) * 512]),
            wkTs=np.ascontiguousarray(wkT[:, r * 512:(r + 1) * 512]),
            wvTs=np.ascontiguousarray(wvT[:, r * 512:(r + 1) * 512]),
            woTs=np.ascontiguousarray(woT[r * 512:(r + 1) * 512, :]),
            su_ug=su_ug, su_dp=su_dp, ws_up=ws_up, ws_gt=ws_gt, ws_dp=ws_dp,
            h128_in=h128c, hi64_in=hi64, tri_in=tri,
        ))
    return in_maps


def kernel(**inputs):
    in_maps = _host_inputs(inputs)
    nc = build_nc("full")
    res = run_bass_kernel_spmd(nc, in_maps, core_ids=list(range(8)))
    out = np.empty((2, 4096, 2048), np.float32)
    for c in range(8):
        b, r = c // 4, c % 4
        out[b, r * 1024:(r + 1) * 1024, :] = res.results[c]["y"].T
    return out
